# version 01
# baseline (speedup 1.0000x reference)
import contextlib
import numpy as np
import ml_dtypes
import concourse.bass as bass
import concourse.mybir as mybir
from concourse.bass_utils import run_bass_kernel_spmd
from concourse.ap import AP

F32 = mybir.dt.float32
BF16 = mybir.dt.bfloat16
I32 = mybir.dt.int32
AF = mybir.ActivationFunctionType
ALU = mybir.AluOpType
AX = mybir.AxisListType

NCORES = 8
S = 8192
D = 2048
TOK = S // NCORES
NT = TOK // 128
ND = D // 128
HEADS = 16
HD = 128
HPC = HEADS // NCORES
DFF0 = 8192
DIN = 4096
DSTATE = 128
NH_M = 64
PH = 64
DXBC = 4352
DIP = 8512
CHUNK = 64
NCHUNK = TOK // CHUNK
HID1 = 5632

ENGS = ("sync", "tensor", "scalar", "vector", "gpsimd")


class Ctx:

    def __init__(self, eng_name, eng):
        self.en = eng_name
        self.eng = eng
        self.c = {}

    def val(self, sem):
        return self.c.get(sem.name, 0)

    def _bump(self, sem, inc):
        self.c[sem.name] = self.c.get(sem.name, 0) + inc
        return self.c[sem.name]

    def do(self, which, f, sem=None, inc=1, wait=()):
        if self.en == which:
            for (s, v) in wait:
                self.eng.wait_ge(s, v)
            ins = f(self.eng)
            if sem is not None:
                ins.then_inc(sem, inc)
        if sem is not None:
            return self._bump(sem, inc)
        return None

    def wait(self, which, sem, v):
        if self.en == which:
            self.eng.wait_ge(sem, v)

    def barrier(self, bar_sem, engines=ENGS):
        n = len(engines)
        base = self.val(bar_sem)
        for e in engines:
            self.do(e, lambda eng: eng.nop(), sem=bar_sem, inc=1)
        for e in engines:
            self.wait(e, bar_sem, base + n)


def build_identity(ctx, g_ident):
    ctx.do("gpsimd", lambda e: e.memset(g_ident[:], 1.0))
    ctx.do("gpsimd", lambda e: e.affine_select(
        out=g_ident[:], in_=g_ident[:], compare_op=ALU.is_equal, fill=0.0,
        base=0, pattern=[[-1, 128]], channel_multiplier=1))


def emit_ln_fm(ctx, sem, x_fm, out_fm, hp, nd=ND, ntok=TOK):
    d_model = nd * 128
    w = lambda: [(sem, ctx.val(sem))]
    tb0, tb1 = hp["tb0"], hp["tb1"]
    abc, bbc = hp["abc"], hp["bbc"]
    row = hp["row"]
    psA, psB = hp["psA"], hp["psB"]
    oc, orow = hp["ones_col_b"], hp["ones_row"]
    for tc in range(ntok // 512):
        def xs(d):
            return x_fm[:, d * ntok + tc * 512: d * ntok + tc * 512 + 512]
        for d in range(nd):
            ctx.do("scalar", lambda e, d=d: e.activation(out=tb0[:], in_=xs(d), func=AF.Identity), sem=sem, wait=w())
            ctx.do("scalar", lambda e, d=d: e.activation(out=tb1[:], in_=xs(d), func=AF.Square), sem=sem, wait=w())
            ctx.do("tensor", lambda e, d=d: e.matmul(psA[0:1, 0:512], oc[:], tb0[:], start=(d == 0), stop=(d == nd - 1)), sem=sem, wait=w())
            ctx.do("tensor", lambda e, d=d: e.matmul(psA[0:1, 512:1024], oc[:], tb1[:], start=(d == 0), stop=(d == nd - 1)), sem=sem, wait=w())
        ctx.do("vector", lambda e: e.tensor_scalar(out=row[0:1, 0:512], in0=psA[0:1, 0:512], scalar1=1.0 / d_model, scalar2=None, op0=ALU.mult), sem=sem, wait=w())
        ctx.do("vector", lambda e: e.tensor_scalar(out=row[0:1, 512:1024], in0=psA[0:1, 512:1024], scalar1=1.0 / d_model, scalar2=None, op0=ALU.mult), sem=sem, wait=w())
        ctx.do("vector", lambda e: e.scalar_tensor_tensor(out=row[0:1, 512:1024], in0=row[0:1, 0:512], scalar=-1.0, in1=row[0:1, 512:1024], op0=ALU.mult if False else ALU.bypass, op1=ALU.bypass) if False else None, sem=None)
        ctx.do("vector", lambda e: e.tensor_tensor(out=row[0:1, 1024:1536], in0=row[0:1, 0:512], in1=row[0:1, 0:512], op=ALU.mult), sem=sem, wait=w())
        ctx.do("vector", lambda e: e.tensor_tensor(out=row[0:1, 512:1024], in0=row[0:1, 512:1024], in1=row[0:1, 1024:1536], op=ALU.subtract), sem=sem, wait=w())
        ctx.do("vector", lambda e: e.tensor_scalar(out=row[0:1, 512:1024], in0=row[0:1, 512:1024], scalar1=1e-5, scalar2=None, op0=ALU.add), sem=sem, wait=w())
        ctx.do("scalar", lambda e: e.activation(out=row[0:1, 512:1024], in_=row[0:1, 512:1024], func=AF.Ln), sem=sem, wait=w())
        ctx.do("scalar", lambda e: e.activation(out=row[0:1, 512:1024], in_=row[0:1, 512:1024], func=AF.Exp, scale=-0.5), sem=sem, wait=w())
        ctx.do("vector", lambda e: e.tensor_tensor(out=row[0:1, 1024:1536], in0=row[0:1, 0:512], in1=row[0:1, 512:1024], op=ALU.mult), sem=sem, wait=w())
        ctx.do("vector", lambda e: e.tensor_scalar(out=row[0:1, 1024:1536], in0=row[0:1, 1024:1536], scalar1=-1.0, scalar2=None, op0=ALU.mult), sem=sem, wait=w())
        ctx.do("tensor", lambda e: e.matmul(psB[:], orow[:], row[0:1, 512:1024], start=True, stop=True), sem=sem, wait=w())
        ctx.do("scalar", lambda e: e.activation(out=abc[:], in_=psB[:], func=AF.Copy), sem=sem, wait=w())
        ctx.do("tensor", lambda e: e.matmul(psB[:], orow[:], row[0:1, 1024:1536], start=True, stop=True), sem=sem, wait=w())
        ctx.do("scalar", lambda e: e.activation(out=bbc[:], in_=psB[:], func=AF.Copy), sem=sem, wait=w())
        for d in range(nd):
            ctx.do("vector", lambda e, d=d: e.tensor_tensor(out=tb0[:] if False else hp["tf"][:], in0=xs(d), in1=abc[:], op=ALU.mult), sem=sem, wait=w())
            ctx.do("vector", lambda e, d=d, tc=tc: e.tensor_tensor(out=out_fm[:, d * ntok + tc * 512: d * ntok + tc * 512 + 512], in0=hp["tf"][:], in1=bbc[:], op=ALU.add), sem=sem, wait=w())


def build_u1():
    nc = bass.Bass("TRN2", target_bir_lowering=False)
    x_d = nc.dram_tensor("x", [D, TOK], F32, kind="ExternalInput").ap()
    lnx_d = nc.dram_tensor("lnx", [D, TOK], BF16, kind="ExternalOutput").ap()

    st = contextlib.ExitStack()
    sb = lambda *a, **k: st.enter_context(nc.sbuf_tensor(*a, **k))
    ps = lambda *a, **k: st.enter_context(nc.psum_tensor(*a, **k))
    x_fm = sb("x_fm", [128, ND * TOK], F32)
    lnxfm = sb("lnxfm", [128, ND * TOK], BF16)
    hp = {
        "tb0": sb("tb0", [128, 512], BF16),
        "tb1": sb("tb1", [128, 512], BF16),
        "tf": sb("tf", [128, 512], F32),
        "abc": sb("abc", [128, 512], F32),
        "bbc": sb("bbc", [128, 512], F32),
        "row": sb("row", [1, 1536], F32),
        "ones_col_b": sb("ones_col_b", [128, 1], BF16),
        "ones_row": sb("ones_row", [1, 128], F32),
        "psA": ps("psA", [1, 1024], F32),
        "psB": ps("psB", [128, 512], F32),
    }
    sem = st.enter_context(nc.semaphore("s"))

    def prog(eng_name):
        def f(eng):
            ctx = Ctx(eng_name, eng)
            w = lambda: [(sem, ctx.val(sem))]
            ctx.do("gpsimd", lambda e: e.memset(hp["ones_col_b"][:], 1.0), sem=sem)
            ctx.do("gpsimd", lambda e: e.memset(hp["ones_row"][:], 1.0), sem=sem, wait=w())
            for d in range(ND):
                ctx.do("sync", lambda e, d=d: e.dma_start(out=x_fm[:, d * TOK:(d + 1) * TOK], in_=x_d[d * 128:(d + 1) * 128, :]), sem=sem, inc=16, wait=w())
            emit_ln_fm(ctx, sem, x_fm, lnxfm, hp)
            done = ctx.val(sem)
            for d in range(ND):
                ctx.do("sync", lambda e, d=d: e.dma_start(out=lnx_d[d * 128:(d + 1) * 128, :], in_=lnxfm[:, d * TOK:(d + 1) * TOK]), sem=sem, inc=16, wait=[(sem, done)])
            ctx.wait("sync", sem, ctx.val(sem))
        return f

    with nc.Block() as block:
        for en in ENGS:
            getattr(block, en)(prog(en))
    st.close()
    return nc


ATT_SCALE = float(HD) ** -0.5


def build_u2():
    nc = bass.Bass("TRN2", target_bir_lowering=False)
    lnx_d = nc.dram_tensor("lnx", [D, S], BF16, kind="ExternalInput").ap()
    wqT_d = nc.dram_tensor("wqT", [D, HPC * HD], F32, kind="ExternalInput").ap()
    wkT_d = nc.dram_tensor("wkT", [D, HPC * HD], F32, kind="ExternalInput").ap()
    wvT_d = nc.dram_tensor("wvT", [D, HPC * HD], F32, kind="ExternalInput").ap()
    bq_d = nc.dram_tensor("bq", [HPC * HD], F32, kind="ExternalInput").ap()
    bk_d = nc.dram_tensor("bk", [HPC * HD], F32, kind="ExternalInput").ap()
    bv_d = nc.dram_tensor("bv", [HPC * HD], F32, kind="ExternalInput").ap()
    attn_d = nc.dram_tensor("attn", [HPC * HD, S], BF16, kind="ExternalOutput").ap()

    st = contextlib.ExitStack()
    sb = lambda *a, **k: st.enter_context(nc.sbuf_tensor(*a, **k))
    ps = lambda *a, **k: st.enter_context(nc.psum_tensor(*a, **k))
    wld_f = sb("wld_f", [128, 256], F32)
    wq_b = sb("wq_b", [128, ND * 256], BF16)
    wk_b = sb("wk_b", [128, ND * 256], BF16)
    wv_b = sb("wv_b", [128, ND * 256], BF16)
    bias_sb = sb("bias_sb", [128, 8], F32)
    bvbc = sb("bvbc", [128, 256], F32)
    lnx_sb = [sb(f"lnx_sb{i}", [128, ND * 512], BF16) for i in range(2)]
    q_fm = sb("q_fm", [128, HPC * S], BF16)
    k_fm = sb("k_fm", [128, HPC * S], BF16)
    v_sb = sb("v_sb", [128, (S // 128) * 260], BF16)
    pt_sb = sb("pt_sb", [128, 1024], BF16)
    attn_sb = sb("attn_sb", [128, HPC * S], BF16)
    atile = sb("atile", [128, 128], BF16)
    rec_sb = sb("rec_sb", [128, 1], F32)
    sq_sb = sb("sq_sb", [128, 2048], BF16)
    scal = sb("scal", [1, 16], F32)
    negb = sb("negb", [128, 2], F32)
    ones_col = sb("ones_col", [128, 1], BF16)
    ones_row = sb("ones_row", [1, 128], F32)
    ident = sb("ident", [128, 128], BF16)

    psS = ps("psS", [128, 1024], F32)
    psK2 = ps("psK2", [128, 1024], F32)
    psV = ps("psV", [128, 1024], F32)
    psT = ps("psT", [128, 128], BF16)
    psM = ps("psM", [128, 512], F32)
    sem = st.enter_context(nc.semaphore("s"))

    NQC = S // 512

    def prog(eng_name):
        def f(eng):
            ctx = Ctx(eng_name, eng)
            w = lambda: [(sem, ctx.val(sem))]

            def chain(which, fns, inc_last=1):
                if ctx.en == which:
                    ctx.eng.wait_ge(sem, ctx.val(sem))
                    for i, fn in enumerate(fns):
                        ins = fn(ctx.eng)
                    ins.then_inc(sem, inc_last)
                ctx._bump(sem, inc_last)

            build_identity(ctx, ident)
            ctx.do("gpsimd", lambda e: e.memset(ones_col[:], 1.0), sem=sem, wait=w())
            ctx.do("gpsimd", lambda e: e.memset(ones_row[:], 1.0), sem=sem, wait=w())
            ctx.do("gpsimd", lambda e: e.memset(v_sb[:], 0.0), sem=sem, wait=w())
            for d in range(ND):
                for (wT_d, w_b) in ((wqT_d, wq_b), (wkT_d, wk_b), (wvT_d, wv_b)):
                    ctx.do("sync", lambda e, d=d, wT_d=wT_d: e.dma_start(out=wld_f[:], in_=wT_d[d * 128:(d + 1) * 128, :]), sem=sem, inc=16, wait=w())
                    ctx.do("vector", lambda e, d=d, w_b=w_b: e.tensor_copy(out=w_b[:, d * 256:(d + 1) * 256], in_=wld_f[:]), sem=sem, wait=w())
            for h in range(2):
                ctx.do("sync", lambda e, h=h: e.dma_start(out=bias_sb[:, h:h + 1], in_=bq_d[h * 128:(h + 1) * 128].unsqueeze(1)), sem=sem, inc=16, wait=w())
                ctx.do("sync", lambda e, h=h: e.dma_start(out=bias_sb[:, 2 + h:3 + h], in_=bk_d[h * 128:(h + 1) * 128].unsqueeze(1)), sem=sem, inc=16, wait=w())
            ctx.do("vector", lambda e: e.tensor_scalar(out=bias_sb[:, 0:2], in0=bias_sb[:, 0:2], scalar1=ATT_SCALE, scalar2=None, op0=ALU.mult), sem=sem, wait=w())
            ctx.do("sync", lambda e: e.dma_start(out=bvbc[0:1, 0:256], in_=bv_d.unsqueeze(0)), sem=sem, inc=16, wait=w())
            ctx.do("tensor", lambda e: e.matmul(psM[:, 0:256], ones_row[:], bvbc[0:1, 0:256], start=True, stop=True), sem=sem, wait=w())
            ctx.do("scalar", lambda e: e.activation(out=bvbc[:], in_=psM[:, 0:256], func=AF.Copy), sem=sem, wait=w())

            for qc in range(NQC):
                buf = lnx_sb[qc % 2]
                for d in range(ND):
                    ctx.do("sync", lambda e, d=d, qc=qc, buf=buf: e.dma_start(out=buf[:, d * 512:(d + 1) * 512], in_=lnx_d[d * 128:(d + 1) * 128, qc * 512:(qc + 1) * 512]), sem=sem, inc=16, wait=w())
                for h in range(2):
                    chain("tensor", [
                        (lambda e, h=h, d=d, buf=buf: e.matmul(psS[:, h * 512:(h + 1) * 512], wq_b[:, d * 256 + h * 128: d * 256 + h * 128 + 128], buf[:, d * 512:(d + 1) * 512], start=(d == 0), stop=(d == ND - 1)))
                        for d in range(ND)])
                    chain("tensor", [
                        (lambda e, h=h, d=d, buf=buf: e.matmul(psK2[:, h * 512:(h + 1) * 512], wk_b[:, d * 256 + h * 128: d * 256 + h * 128 + 128], buf[:, d * 512:(d + 1) * 512], start=(d == 0), stop=(d == ND - 1)))
                        for d in range(ND)])
                    ctx.do("scalar", lambda e, h=h, qc=qc: e.activation(out=q_fm[:, h * S + qc * 512: h * S + qc * 512 + 512], in_=psS[:, h * 512:(h + 1) * 512], func=AF.Identity, scale=ATT_SCALE, bias=bias_sb[:, h:h + 1]), sem=sem, wait=w())
                    ctx.do("scalar", lambda e, h=h, qc=qc: e.activation(out=k_fm[:, h * S + qc * 512: h * S + qc * 512 + 512], in_=psK2[:, h * 512:(h + 1) * 512], func=AF.Identity, scale=1.0, bias=bias_sb[:, 2 + h:3 + h]), sem=sem, wait=w())
                for j in range(4):
                    chain("tensor", [
                        (lambda e, j=j, d=d, buf=buf: e.matmul(psV[:, (j % 2) * 512: (j % 2) * 512 + 256] if False else psV[:, j * 256:(j + 1) * 256], buf[:, d * 512 + j * 128: d * 512 + j * 128 + 128], wv_b[:, d * 256:(d + 1) * 256], start=(d == 0), stop=(d == ND - 1)))
                        for d in range(ND)])
                    ctx.do("vector", lambda e, j=j: e.tensor_add(out=psV[:, j * 256:(j + 1) * 256], in0=psV[:, j * 256:(j + 1) * 256], in1=bvbc[:]), sem=sem, wait=w())
                    tt = qc * 4 + j
                    for h in range(2):
                        ctx.do("vector", lambda e, j=j, h=h, tt=tt: e.tensor_copy(out=v_sb[:, tt * 260 + h * 129: tt * 260 + h * 129 + 128], in_=psV[:, j * 256 + h * 128: j * 256 + (h + 1) * 128]), sem=sem, wait=w())
                        ctx.do("gpsimd", lambda e, tt=tt, h=h: e.memset(v_sb[:, tt * 260 + h * 129 + 128: tt * 260 + h * 129 + 129], 1.0), sem=sem, wait=w())

            for h in range(2):
                for (src, outc) in ((q_fm, 0), (k_fm, 1)):
                    for ch in range(S // 2048):
                        ctx.do("scalar", lambda e, h=h, ch=ch, src=src: e.activation(out=sq_sb[:], in_=src[:, h * S + ch * 2048: h * S + ch * 2048 + 2048], func=AF.Square), sem=sem, wait=w())
                        for sub in range(4):
                            ctx.do("tensor", lambda e, sub=sub: e.matmul(psM[:, 0:512] if False else psM[0:1, 0:512], ones_col[:], sq_sb[:, sub * 512:(sub + 1) * 512], start=True, stop=True), sem=sem, wait=w())
                            first = (ch == 0 and sub == 0)
                            if first:
                                ctx.do("vector", lambda e, outc=outc, h=h: e.reduce_max(out=scal[0:1, 8 + outc:9 + outc], in_=psM[0:1, 0:512], axis=AX.X), sem=sem, wait=w())
                                ctx.do("vector", lambda e, outc=outc, h=h: e.tensor_copy(out=scal[0:1, outc:outc + 1], in_=scal[0:1, 8 + outc:9 + outc]), sem=sem, wait=w())
                            else:
                                ctx.do("vector", lambda e, outc=outc: e.reduce_max(out=scal[0:1, 8 + outc:9 + outc], in_=psM[0:1, 0:512], axis=AX.X), sem=sem, wait=w())
                                ctx.do("vector", lambda e, outc=outc: e.tensor_tensor(out=scal[0:1, outc:outc + 1], in0=scal[0:1, outc:outc + 1], in1=scal[0:1, 8 + outc:9 + outc], op=ALU.max), sem=sem, wait=w())
                ctx.do("scalar", lambda e: e.activation(out=scal[0:1, 2:3], in_=scal[0:1, 0:1], func=AF.Ln), sem=sem, wait=w())
                ctx.do("scalar", lambda e: e.activation(out=scal[0:1, 3:4], in_=scal[0:1, 1:2], func=AF.Ln), sem=sem, wait=w())
                ctx.do("vector", lambda e: e.tensor_add(out=scal[0:1, 4:5], in0=scal[0:1, 2:3], in1=scal[0:1, 3:4]), sem=sem, wait=w())
                ctx.do("scalar", lambda e: e.activation(out=scal[0:1, 5:6], in_=scal[0:1, 4:5], func=AF.Exp, scale=0.5), sem=sem, wait=w())
                ctx.do("vector", lambda e: e.tensor_scalar(out=scal[0:1, 6:7], in0=scal[0:1, 5:6], scalar1=-1.0, scalar2=None, op0=ALU.mult), sem=sem, wait=w())
                ctx.do("tensor", lambda e: e.matmul(psM[:, 0:1], ones_row[:], scal[0:1, 6:7], start=True, stop=True), sem=sem, wait=w())
                ctx.do("scalar", lambda e, h=h: e.activation(out=negb[:, h:h + 1], in_=psM[:, 0:1], func=AF.Copy), sem=sem, wait=w())

            for h in range(2):
                for qc in range(NQC):
                    kvt = 4 * (qc + 1)
                    for pair in range(kvt // 2):
                        k0 = 2 * pair
                        chain("tensor", [
                            (lambda e, h=h, qc=qc, kt=k0 + half, half=half: e.matmul(psS[:, half * 512:(half + 1) * 512], k_fm[:, h * S + (k0 + half) * 128: h * S + (k0 + half) * 128 + 128], q_fm[:, h * S + qc * 512: h * S + qc * 512 + 512], start=True, stop=True))
                            for half in range(2)])
                        ctx.do("scalar", lambda e, h=h: e.activation(out=pt_sb[:], in_=psS[:], func=AF.Exp, bias=negb[:, h:h + 1]), sem=sem, wait=w())
                        for half in range(2):
                            kt = k0 + half
                            if kt * 128 + 127 >= qc * 512:
                                ctx.do("gpsimd", lambda e, kt=kt, qc=qc, half=half: e.affine_select(
                                    out=pt_sb[:, half * 512:(half + 1) * 512], in_=pt_sb[:, half * 512:(half + 1) * 512],
                                    compare_op=ALU.is_ge, fill=0.0,
                                    base=qc * 512 - kt * 128, pattern=[[1, 512]], channel_multiplier=-1), sem=sem, wait=w())
                        av = []
                        for half in range(2):
                            kt = k0 + half
                            for j in range(4):
                                ob = (psK2, j) if j < 2 else (psV, j - 2)
                                av.append((lambda e, h=h, kt=kt, half=half, j=j, ob=ob: e.matmul(
                                    ob[0][:, ob[1] * 512: ob[1] * 512 + 129],
                                    pt_sb[:, half * 512 + j * 128: half * 512 + j * 128 + 128],
                                    v_sb[:, kt * 260 + h * 129: kt * 260 + h * 129 + 129],
                                    start=(kt == 0), stop=(kt == kvt - 1))))
                        chain("tensor", av)
                    for j in range(4):
                        ob = (psK2, j) if j < 2 else (psV, j - 2)
                        ctx.do("vector", lambda e, ob=ob: e.reciprocal(out=rec_sb[:], in_=ob[0][:, ob[1] * 512 + 128: ob[1] * 512 + 129]), sem=sem, wait=w())
                        ctx.do("vector", lambda e, ob=ob: e.tensor_scalar(out=atile[:], in0=ob[0][:, ob[1] * 512: ob[1] * 512 + 128], scalar1=rec_sb[:], scalar2=None, op0=ALU.mult), sem=sem, wait=w())
                        ctx.do("tensor", lambda e: e.transpose(psT[:], atile[:], ident[:]), sem=sem, wait=w())
                        ctx.do("vector", lambda e, h=h, qc=qc, j=j: e.tensor_copy(out=attn_sb[:, h * S + qc * 512 + j * 128: h * S + qc * 512 + j * 128 + 128], in_=psT[:]), sem=sem, wait=w())

            done = ctx.val(sem)
            for h in range(2):
                ctx.do("sync", lambda e, h=h: e.dma_start(out=attn_d[h * 128:(h + 1) * 128, :], in_=attn_sb[:, h * S:(h + 1) * S]), sem=sem, inc=16, wait=[(sem, done)])
            ctx.wait("sync", sem, ctx.val(sem))
        return f

    with nc.Block() as block:
        for en in ENGS:
            getattr(block, en)(prog(en))
    st.close()
    return nc


def build_u3():
    nc = bass.Bass("TRN2", target_bir_lowering=False)
    x_d = nc.dram_tensor("x", [D, TOK], F32, kind="ExternalInput").ap()
    attn_d = nc.dram_tensor("attn", [D, TOK], BF16, kind="ExternalInput").ap()
    woT_d = nc.dram_tensor("woT", [D, D], F32, kind="ExternalInput").ap()
    bo_d = nc.dram_tensor("bo", [D], F32, kind="ExternalInput").ap()
    w1T_d = nc.dram_tensor("w1T", [D, DFF0], F32, kind="ExternalInput").ap()
    b1_d = nc.dram_tensor("b1", [DFF0], F32, kind="ExternalInput").ap()
    w2T_d = nc.dram_tensor("w2T", [DFF0, D], F32, kind="ExternalInput").ap()
    b2_d = nc.dram_tensor("b2", [D], F32, kind="ExternalInput").ap()
    x2_d = nc.dram_tensor("x2", [D, TOK], F32, kind="ExternalOutput").ap()
    dbg_d = nc.dram_tensor("dbg", [D, TOK], F32, kind="ExternalOutput").ap()
    dbg2_d = nc.dram_tensor("dbg2", [128, 8], F32, kind="ExternalOutput").ap()
    dbg3_d = nc.dram_tensor("dbg3", [128, 32768], BF16, kind="ExternalOutput").ap()

    st = contextlib.ExitStack()
    sb = lambda *a, **k: st.enter_context(nc.sbuf_tensor(*a, **k))
    ps = lambda *a, **k: st.enter_context(nc.psum_tensor(*a, **k))
    x_fm = sb("x_fm", [128, ND * TOK], F32)
    A = sb("A", [128, ND * TOK], BF16)
    B = sb("B", [128, 32768], BF16)
    wq_sb = sb("wq_sb", [128, 8192], BF16)
    wf_sb = sb("wf_sb", [128, 2048], F32)
    m_sb = sb("m_sb", [128, 2048], BF16)
    tf_sb = sb("tf_sb", [128, 512], F32)
    bcol = sb("bcol", [128, 96], F32)
    acc = sb("acc", [128, 4], F32)
    thr = sb("thr", [128, 4], F32)
    scal = sb("scal", [1, 8], F32)
    nrow = sb("nrow", [1, 512], BF16)
    nrow_lo = sb("nrow_lo", [1, 512], BF16)
    nrow_f = sb("nrow_f", [1, 512], F32)
    hp = {
        "tb0": sb("tb0", [128, 512], BF16),
        "tb1": sb("tb1", [128, 512], BF16),
        "tf": tf_sb,
        "abc": sb("abc", [128, 512], F32),
        "bbc": sb("bbc", [128, 512], F32),
        "row": sb("row", [1, 1536], F32),
        "ones_col_b": sb("ones_col_b", [128, 1], BF16),
        "ones_row": sb("ones_row", [1, 128], F32),
    }
    ones_col_f = sb("ones_col_f", [128, 1], F32)
    ones_row_b = sb("ones_row_b", [1, 128], BF16)
    psA = ps("psA", [1, 1024], F32)
    psB = ps("psB", [128, 512], F32)
    psC = ps("psC", [128, 512], F32)
    hp["psA"] = psA
    hp["psB"] = psB
    sem = st.enter_context(nc.semaphore("s"))

    def prog(eng_name):
        def f(eng):
            ctx = Ctx(eng_name, eng)
            w = lambda: [(sem, ctx.val(sem))]

            def chain(which, fns, inc_last=1):
                if ctx.en == which:
                    ctx.eng.wait_ge(sem, ctx.val(sem))
                    for fn in fns:
                        ins = fn(ctx.eng)
                    ins.then_inc(sem, inc_last)
                ctx._bump(sem, inc_last)

            ctx.do("gpsimd", lambda e: e.memset(hp["ones_col_b"][:], 1.0), sem=sem)
            ctx.do("gpsimd", lambda e: e.memset(hp["ones_row"][:], 1.0), sem=sem, wait=w())
            ctx.do("gpsimd", lambda e: e.memset(ones_col_f[:], 1.0), sem=sem, wait=w())
            ctx.do("gpsimd", lambda e: e.memset(ones_row_b[:], 1.0), sem=sem, wait=w())
            ctx.do("gpsimd", lambda e: e.memset(acc[:], 0.0), sem=sem, wait=w())
            for d in range(ND):
                ctx.do("sync", lambda e, d=d: e.dma_start(out=x_fm[:, d * TOK:(d + 1) * TOK], in_=x_d[d * 128:(d + 1) * 128, :]), sem=sem, inc=16, wait=w())
                ctx.do("sync", lambda e, d=d: e.dma_start(out=A[:, d * TOK:(d + 1) * TOK], in_=attn_d[d * 128:(d + 1) * 128, :]), sem=sem, inc=16, wait=w())
                ctx.do("sync", lambda e, d=d: e.dma_start(out=bcol[:, d:d + 1], in_=bo_d[d * 128:(d + 1) * 128].unsqueeze(1)), sem=sem, inc=16, wait=w())
                ctx.do("sync", lambda e, d=d: e.dma_start(out=bcol[:, 16 + d:17 + d], in_=b2_d[d * 128:(d + 1) * 128].unsqueeze(1)), sem=sem, inc=16, wait=w())
            for ff in range(64):
                ctx.do("sync", lambda e, ff=ff: e.dma_start(out=bcol[:, 32 + ff:33 + ff], in_=b1_d[ff * 128:(ff + 1) * 128].unsqueeze(1)), sem=sem, inc=16, wait=w())
            for dp in range(ND):
                ctx.do("sync", lambda e, dp=dp: e.dma_start(out=wf_sb[:, 0:2048], in_=woT_d[dp * 128:(dp + 1) * 128, :]), sem=sem, inc=16, wait=w())
                ctx.do("vector", lambda e, dp=dp: e.tensor_copy(out=B[:, dp * 2048:(dp + 1) * 2048], in_=wf_sb[:, 0:2048]), sem=sem, wait=w())

            for ft in range(ND):
                for tc in range(2):
                    chain("tensor", [
                        (lambda e, dp=dp, ft=ft, tc=tc: e.matmul(psC[:], B[:, dp * 2048 + ft * 128: dp * 2048 + ft * 128 + 128], A[:, dp * TOK + tc * 512: dp * TOK + tc * 512 + 512], start=(dp == 0), stop=(dp == ND - 1)))
                        for dp in range(ND)])
                    ctx.do("scalar", lambda e, ft=ft: e.activation(out=tf_sb[:], in_=psC[:], func=AF.Identity, scale=1.0, bias=bcol[:, ft:ft + 1]), sem=sem, wait=w())
                    ctx.do("vector", lambda e, ft=ft, tc=tc: e.tensor_add(out=x_fm[:, ft * TOK + tc * 512: ft * TOK + tc * 512 + 512], in0=x_fm[:, ft * TOK + tc * 512: ft * TOK + tc * 512 + 512], in1=tf_sb[:]), sem=sem, wait=w())

            dd = ctx.val(sem)
            for d in range(ND):
                ctx.do("sync", lambda e, d=d: e.dma_start(out=dbg_d[d * 128:(d + 1) * 128, :], in_=x_fm[:, d * TOK:(d + 1) * TOK]), sem=sem, inc=16, wait=[(sem, dd)])
            emit_ln_fm(ctx, sem, x_fm, A, hp)

            for (wT, colacc, rows, cols) in ((w1T_d, 0, D, DFF0), (w2T_d, 1, DFF0, D)):
                for r in range(rows // 128):
                    for cc in range(cols // 2048):
                        ctx.do("sync", lambda e, r=r, cc=cc, wT=wT: e.dma_start(out=wf_sb[:], in_=wT[r * 128:(r + 1) * 128, cc * 2048:(cc + 1) * 2048]), sem=sem, inc=16, wait=w())
                        ctx.do("vector", lambda e: e.reduce_sum(out=acc[:, 2:3], in_=wf_sb[:], axis=AX.X, apply_absolute_value=True), sem=sem, wait=w())
                        ctx.do("vector", lambda e, colacc=colacc: e.tensor_add(out=acc[:, colacc:colacc + 1], in0=acc[:, colacc:colacc + 1], in1=acc[:, 2:3]), sem=sem, wait=w())
            for i, nelem in ((0, D * DFF0), (1, DFF0 * D)):
                ctx.do("tensor", lambda e, i=i: e.matmul(psA[0:1, 0:1], ones_col_f[:], acc[:, i:i + 1], start=True, stop=True), sem=sem, wait=w())
                ctx.do("vector", lambda e, i=i, nelem=nelem: e.tensor_scalar(out=scal[0:1, i:i + 1], in0=psA[0:1, 0:1], scalar1=0.7 / nelem, scalar2=None, op0=ALU.mult), sem=sem, wait=w())
                for (sgn, col) in ((1.0, 2 * i), (-1.0, 2 * i + 1)):
                    ctx.do("vector", lambda e, i=i, sgn=sgn: e.tensor_scalar(out=scal[0:1, 4:5], in0=scal[0:1, i:i + 1], scalar1=sgn, scalar2=None, op0=ALU.mult), sem=sem, wait=w())
                    ctx.do("tensor", lambda e: e.matmul(psA[:, 0:1].tensor[0:128, 0:1] if False else psB[:, 0:1], hp["ones_row"][:], scal[0:1, 4:5], start=True, stop=True), sem=sem, wait=w())
                    ctx.do("scalar", lambda e, col=col: e.activation(out=thr[:, col:col + 1], in_=psB[:, 0:1], func=AF.Copy), sem=sem, wait=w())

            ctx.do("sync", lambda e: e.dma_start(out=dbg2_d[:, 0:4], in_=thr[:, 0:4]), sem=sem, inc=16, wait=w())
            for half in range(2):
                hoff = half * 512
                chain("tensor", [
                    (lambda e, d=d: e.matmul(psA[0:1, 0:512], hp["ones_col_b"][:], A[:, d * TOK + hoff: d * TOK + hoff + 512], start=(d == 0), stop=(d == ND - 1)))
                    for d in range(ND)])
                ctx.do("scalar", lambda e: e.activation(out=nrow_f[0:1, :], in_=psA[0:1, 0:512], func=AF.Copy, scale=-1.0), sem=sem, wait=w())
                ctx.do("vector", lambda e: e.tensor_copy(out=nrow[0:1, :], in_=nrow_f[0:1, :]), sem=sem, wait=w())
                ctx.do("vector", lambda e: e.tensor_tensor(out=nrow_lo[0:1, :], in0=nrow_f[0:1, :], in1=nrow[0:1, :], op=ALU.subtract), sem=sem, wait=w())
                for fb in range(16):
                    for dp in range(ND):
                        ctx.do("sync", lambda e, fb=fb, dp=dp: e.dma_start(out=wf_sb[:, 0:512], in_=w1T_d[dp * 128:(dp + 1) * 128, fb * 512:(fb + 1) * 512]), sem=sem, inc=16, wait=w())
                        ctx.do("vector", lambda e: e.tensor_scalar(out=m_sb[:, 0:512], in0=wf_sb[:, 0:512], scalar1=thr[:, 0:1], scalar2=None, op0=ALU.is_ge), sem=sem, wait=w())
                        ctx.do("vector", lambda e: e.tensor_scalar(out=m_sb[:, 512:1024], in0=wf_sb[:, 0:512], scalar1=thr[:, 1:2], scalar2=None, op0=ALU.is_ge), sem=sem, wait=w())
                        ctx.do("vector", lambda e, dp=dp: e.tensor_add(out=wq_sb[:, dp * 512:(dp + 1) * 512], in0=m_sb[:, 0:512], in1=m_sb[:, 512:1024]), sem=sem, wait=w())
                    for fs in range(4):
                        ff = fb * 4 + fs
                        mms = [
                            (lambda e, dp=dp, fs=fs: e.matmul(psC[:], wq_sb[:, dp * 512 + fs * 128: dp * 512 + fs * 128 + 128], A[:, dp * TOK + hoff: dp * TOK + hoff + 512], start=(dp == 0), stop=False))
                            for dp in range(ND)]
                        mms.append(lambda e: e.matmul(psC[:], ones_row_b[:], nrow[0:1, :], start=False, stop=False))
                        mms.append(lambda e: e.matmul(psC[:], ones_row_b[:], nrow_lo[0:1, :], start=False, stop=True))
                        chain("tensor", mms)
                        ctx.do("scalar", lambda e, ff=ff: e.activation(out=B[:, ff * 512:(ff + 1) * 512], in_=psC[:], func=AF.Gelu, bias=bcol[:, 32 + ff:33 + ff]), sem=sem, wait=w())
                if half == 0:
                    d3 = ctx.val(sem)
                    ctx.do("sync", lambda e: e.dma_start(out=dbg3_d[:], in_=B[:]), sem=sem, inc=16, wait=[(sem, d3)])
                for ft in range(ND):
                    for sub in range(4):
                        ctx.do("sync", lambda e, ft=ft, sub=sub: e.dma_start(
                            out=wf_sb[:].rearrange("p (a c) -> p a c", c=128),
                            in_=w2T_d[sub * 2048:(sub + 1) * 2048, ft * 128:(ft + 1) * 128].rearrange("(a p) c -> p a c", p=128)), sem=sem, inc=16, wait=w())
                        ctx.do("vector", lambda e: e.tensor_scalar(out=m_sb[:], in0=wf_sb[:], scalar1=thr[:, 2:3], scalar2=None, op0=ALU.is_ge), sem=sem, wait=w())
                        ctx.do("vector", lambda e, sub=sub: e.tensor_scalar(out=wq_sb[:, sub * 2048:(sub + 1) * 2048], in0=wf_sb[:], scalar1=thr[:, 3:4], scalar2=None, op0=ALU.is_ge), sem=sem, wait=w())
                        ctx.do("vector", lambda e, sub=sub: e.tensor_add(out=wq_sb[:, sub * 2048:(sub + 1) * 2048], in0=wq_sb[:, sub * 2048:(sub + 1) * 2048], in1=m_sb[:]), sem=sem, wait=w())
                    chain("tensor", [
                        (lambda e, f=f: e.matmul(psA[0:1, 0:512], hp["ones_col_b"][:], B[:, f * 512:(f + 1) * 512], start=(f == 0), stop=(f == 63)))
                        for f in range(64)])
                    ctx.do("scalar", lambda e: e.activation(out=nrow_f[0:1, :], in_=psA[0:1, 0:512], func=AF.Copy, scale=-1.0), sem=sem, wait=w())
                    ctx.do("vector", lambda e: e.tensor_copy(out=nrow[0:1, :], in_=nrow_f[0:1, :]), sem=sem, wait=w())
                    ctx.do("vector", lambda e: e.tensor_tensor(out=nrow_lo[0:1, :], in0=nrow_f[0:1, :], in1=nrow[0:1, :], op=ALU.subtract), sem=sem, wait=w())
                    mms = [
                        (lambda e, f=f, ft=ft: e.matmul(psC[:], wq_sb[:, f * 128:(f + 1) * 128], B[:, f * 512:(f + 1) * 512], start=(f == 0), stop=False))
                        for f in range(64)]
                    mms.append(lambda e: e.matmul(psC[:], ones_row_b[:], nrow[0:1, :], start=False, stop=False))
                    mms.append(lambda e: e.matmul(psC[:], ones_row_b[:], nrow_lo[0:1, :], start=False, stop=True))
                    chain("tensor", mms)
                    ctx.do("scalar", lambda e, ft=ft: e.activation(out=tf_sb[:], in_=psC[:], func=AF.Identity, scale=1.0, bias=bcol[:, 16 + ft:17 + ft]), sem=sem, wait=w())
                    ctx.do("vector", lambda e, ft=ft, hoff=hoff: e.tensor_add(out=x_fm[:, ft * TOK + hoff: ft * TOK + hoff + 512], in0=x_fm[:, ft * TOK + hoff: ft * TOK + hoff + 512], in1=tf_sb[:]), sem=sem, wait=w())
                    ctx.do("sync", lambda e, ft=ft, hoff=hoff: e.dma_start(out=x2_d[ft * 128:(ft + 1) * 128, hoff:hoff + 512], in_=x_fm[:, ft * TOK + hoff: ft * TOK + hoff + 512]), sem=sem, inc=16, wait=w())
            ctx.wait("sync", sem, ctx.val(sem))
        return f

    with nc.Block() as block:
        for en in ENGS:
            getattr(block, en)(prog(en))
    st.close()
    return nc


def emit_ln_fm2(ctx, sem, x_src, out_fm, hp, tc_list, in_stride, out_stride, nd=ND):
    d_model = nd * 128
    w = lambda: [(sem, ctx.val(sem))]
    tb0, tb1, abc, bbc, row = hp["tb0"], hp["tb1"], hp["abc"], hp["bbc"], hp["row"]
    psA, psB = hp["psA"], hp["psB"]
    oc, orow = hp["ones_col_b"], hp["ones_row"]
    for tc in tc_list:
        for d in range(nd):
            ctx.do("scalar", lambda e, d=d, tc=tc: e.activation(out=tb0[:], in_=x_src(d, tc), func=AF.Identity), sem=sem, wait=w())
            ctx.do("scalar", lambda e, d=d, tc=tc: e.activation(out=tb1[:], in_=x_src(d, tc), func=AF.Square), sem=sem, wait=w())
            ctx.do("tensor", lambda e, d=d: e.matmul(psA[0:1, 0:512], oc[:], tb0[:], start=(d == 0), stop=(d == nd - 1)), sem=sem, wait=w())
            ctx.do("tensor", lambda e, d=d: e.matmul(psA[0:1, 512:1024], oc[:], tb1[:], start=(d == 0), stop=(d == nd - 1)), sem=sem, wait=w())
        ctx.do("vector", lambda e: e.tensor_scalar(out=row[0:1, 0:512], in0=psA[0:1, 0:512], scalar1=1.0 / d_model, scalar2=None, op0=ALU.mult), sem=sem, wait=w())
        ctx.do("vector", lambda e: e.tensor_scalar(out=row[0:1, 512:1024], in0=psA[0:1, 512:1024], scalar1=1.0 / d_model, scalar2=None, op0=ALU.mult), sem=sem, wait=w())
        ctx.do("vector", lambda e: e.tensor_tensor(out=row[0:1, 1024:1536], in0=row[0:1, 0:512], in1=row[0:1, 0:512], op=ALU.mult), sem=sem, wait=w())
        ctx.do("vector", lambda e: e.tensor_tensor(out=row[0:1, 512:1024], in0=row[0:1, 512:1024], in1=row[0:1, 1024:1536], op=ALU.subtract), sem=sem, wait=w())
        ctx.do("vector", lambda e: e.tensor_scalar(out=row[0:1, 512:1024], in0=row[0:1, 512:1024], scalar1=1e-5, scalar2=None, op0=ALU.add), sem=sem, wait=w())
        ctx.do("scalar", lambda e: e.activation(out=row[0:1, 512:1024], in_=row[0:1, 512:1024], func=AF.Ln), sem=sem, wait=w())
        ctx.do("scalar", lambda e: e.activation(out=row[0:1, 512:1024], in_=row[0:1, 512:1024], func=AF.Exp, scale=-0.5), sem=sem, wait=w())
        ctx.do("vector", lambda e: e.tensor_tensor(out=row[0:1, 1024:1536], in0=row[0:1, 0:512], in1=row[0:1, 512:1024], op=ALU.mult), sem=sem, wait=w())
        ctx.do("vector", lambda e: e.tensor_scalar(out=row[0:1, 1024:1536], in0=row[0:1, 1024:1536], scalar1=-1.0, scalar2=None, op0=ALU.mult), sem=sem, wait=w())
        ctx.do("tensor", lambda e: e.matmul(psB[:, 0:512], orow[:], row[0:1, 512:1024], start=True, stop=True), sem=sem, wait=w())
        ctx.do("scalar", lambda e: e.activation(out=abc[:], in_=psB[:, 0:512], func=AF.Copy), sem=sem, wait=w())
        ctx.do("tensor", lambda e: e.matmul(psB[:, 0:512], orow[:], row[0:1, 1024:1536], start=True, stop=True), sem=sem, wait=w())
        ctx.do("scalar", lambda e: e.activation(out=bbc[:], in_=psB[:, 0:512], func=AF.Copy), sem=sem, wait=w())
        for d in range(nd):
            ctx.do("vector", lambda e, d=d, tc=tc: e.tensor_tensor(out=hp["tf"][:], in0=x_src(d, tc), in1=abc[:], op=ALU.mult), sem=sem, wait=w())
            ctx.do("vector", lambda e, d=d, tc=tc: e.tensor_tensor(out=out_fm[:, d * out_stride + tc * 512: d * out_stride + tc * 512 + 512], in0=hp["tf"][:], in1=bbc[:], op=ALU.add), sem=sem, wait=w())


def build_u5(mode):
    nc = bass.Bass("TRN2", target_bir_lowering=False)
    x_d = nc.dram_tensor("x", [D, TOK], F32, kind="ExternalInput").ap()
    xh_d = nc.dram_tensor("xh", [D, 3], F32, kind="ExternalInput").ap()
    ipT_d = nc.dram_tensor("ipT", [D, DIP], F32, kind="ExternalInput").ap()
    cw_d = nc.dram_tensor("cw", [DXBC, 4], F32, kind="ExternalInput").ap()
    cb_d = nc.dram_tensor("cb", [DXBC], F32, kind="ExternalInput").ap()
    alog_d = nc.dram_tensor("alog", [NH_M], F32, kind="ExternalInput").ap()
    dtb_d = nc.dram_tensor("dtb", [NH_M], F32, kind="ExternalInput").ap()
    if mode == "b":
        dvec_d = nc.dram_tensor("dvec", [NH_M], F32, kind="ExternalInput").ap()
        nw_d = nc.dram_tensor("nw", [DIN], F32, kind="ExternalInput").ap()
        opT_d = nc.dram_tensor("opT", [DIN, D], F32, kind="ExternalInput").ap()
        sf_d = nc.dram_tensor("sf", [NCORES * 128, DIN], F32, kind="ExternalInput").ap()
        phiT_d = nc.dram_tensor("phiT", [NH_M, NCORES], F32, kind="ExternalInput").ap()
        mask_d = nc.dram_tensor("maskr", [NH_M, NCORES], F32, kind="ExternalInput").ap()
        ws_scr = nc.dram_tensor("ws_scr", [8, 64], F32, kind="Internal").ap()
        x3_d = nc.dram_tensor("x3", [D, TOK], F32, kind="ExternalOutput").ap()
        dbgm_d = nc.dram_tensor("dbgm", [128, 8192], F32, kind="ExternalOutput").ap()
    else:
        sfin_d = nc.dram_tensor("sfin", [128, DIN], F32, kind="ExternalOutput").ap()
        phi_d = nc.dram_tensor("phi", [NH_M, 1], F32, kind="ExternalOutput").ap()
    acs_scr = nc.dram_tensor("acs_scr", [NH_M, CHUNK], F32, kind="Internal").ap()
    xssm_scr = nc.dram_tensor("xssm_scr", [TOK, DIN], BF16, kind="Internal").ap()
    z_scr = nc.dram_tensor("z_scr", [TOK, DIN], BF16, kind="Internal").ap() if mode == "b" else None

    st = contextlib.ExitStack()
    sb = lambda *a, **k: st.enter_context(nc.sbuf_tensor(*a, **k))
    ps = lambda *a, **k: st.enter_context(nc.psum_tensor(*a, **k))
    A32 = sb("A32", [128, 16384], BF16)
    XCH = sb("XCH", [128, 8192], F32)
    BIG1 = sb("BIG1", [128, 4096], F32)
    BIG2 = sb("BIG2", [128, 4096], BF16)
    BIG3 = sb("BIG3", [128, 4096], BF16)
    BIG4 = sb("BIG4", [128, 4096], BF16)
    y_tile = sb("y_tile", [128, 4096], BF16)
    hp = {
        "tb0": sb("tb0", [128, 512], BF16),
        "tb1": sb("tb1", [128, 512], BF16),
        "tf": sb("tf", [128, 512], F32),
        "abc": sb("abc", [128, 512], F32),
        "bbc": sb("bbc", [128, 512], F32),
        "row": sb("row", [1, 1536], F32),
        "ones_col_b": sb("ones_col_b", [128, 1], BF16),
        "ones_row": sb("ones_row", [1, 128], F32),
    }
    identb = sb("identb", [128, 128], BF16)
    identf = sb("identf", [128, 128], F32)
    xbcr = sb("xbcr", [128, 1028], BF16)
    xbs = sb("xbs", [128, 1024], BF16)
    BC = sb("BC", [128, 2 * TOK], BF16)
    dt_fm = sb("dt_fm", [64, TOK], F32)
    adisc = sb("adisc", [64, TOK], F32)
    acs = sb("acs", [64, TOK], F32)
    cw_sb = sb("cw_sb", [128, 136], F32)
    cb_sb = sb("cb_sb", [128, 34], F32)
    small = sb("small", [64, 16], F32)
    acsT = sb("acsT", [64, 64], F32)
    eacsT = sb("eacsT", [64, 64], F32)
    ad63 = sb("ad63", [64, 64], F32)
    w_sh = sb("w_sh", [64, 64], F32)
    dtT = sb("dtT", [64, 64], F32)
    btm = sb("btm", [64, 128], BF16)
    gt = sb("gt", [64, 64], BF16)
    er63 = sb("er63", [1, 64], F32)
    E63 = sb("E63", [128, 64], F32)
    D64 = sb("D64", [128, 64], F32)
    S_run = sb("S_run", [128, DIN], F32)
    st_tmp = sb("st_tmp", [128, 2048], F32)
    syb = sb("syb", [128, DIN], BF16)
    x2s = sb("x2s", [128, 512], F32)
    lnxh = sb("lnxh", [128, ND * 4], BF16)
    xht = sb("xht", [128, ND * 4], F32)
    rs_col = sb("rs_col", [128, 2], F32)
    ph8 = sb("ph8", [64, 32], F32)
    ws8 = sb("ws8", [8, 64], F32)
    wsrow = sb("wsrow", [1, 512], F32)
    zt = sb("zt", [128, 1024], BF16) if mode == "b" else None

    PSA = ps("PSA", [128, 2048], F32)
    PSB = ps("PSB", [128, 2048], F32)
    psC = PSA[:, 0:512]
    psA_ln = PSB[0:1, 0:1024]
    psB_ln = PSB[:, 1024:1536]
    psT = PSB[:, 512:576].bitcast(BF16)
    psF = PSB[:, 576:704]
    hp["psA"] = psA_ln
    hp["psB"] = psB_ln
    sem = st.enter_context(nc.semaphore("s"))

    def prog(eng_name):
        def f(eng):
            ctx = Ctx(eng_name, eng)
            w = lambda: [(sem, ctx.val(sem))]

            def chain(which, fns, inc_last=1):
                if ctx.en == which:
                    ctx.eng.wait_ge(sem, ctx.val(sem))
                    for fn in fns:
                        ins = fn(ctx.eng)
                    ins.then_inc(sem, inc_last)
                ctx._bump(sem, inc_last)

            def do(which, f2):
                ctx.do(which, f2, sem=sem, wait=w())

            def dma(f2, which="sync"):
                ctx.do(which, f2, sem=sem, inc=16, wait=w())

            do("gpsimd", lambda e: e.memset(hp["ones_col_b"][:], 1.0))
            do("gpsimd", lambda e: e.memset(hp["ones_row"][:], 1.0))
            do("gpsimd", lambda e: e.memset(identb[:], 1.0))
            do("gpsimd", lambda e: e.affine_select(out=identb[:], in_=identb[:], compare_op=ALU.is_equal, fill=0.0, base=0, pattern=[[-1, 128]], channel_multiplier=1))
            do("gpsimd", lambda e: e.memset(identf[:], 1.0))
            do("gpsimd", lambda e: e.affine_select(out=identf[:], in_=identf[:], compare_op=ALU.is_equal, fill=0.0, base=0, pattern=[[-1, 128]], channel_multiplier=1))
            for ct in range(34):
                dma(lambda e, ct=ct: e.dma_start(out=cw_sb[:, ct * 4:(ct + 1) * 4], in_=cw_d[ct * 128:(ct + 1) * 128, :]))
                dma(lambda e, ct=ct: e.dma_start(out=cb_sb[:, ct:ct + 1], in_=cb_d[ct * 128:(ct + 1) * 128].unsqueeze(1)))
            dma(lambda e: e.dma_start(out=small[:, 0:1], in_=alog_d[:].unsqueeze(1)))
            dma(lambda e: e.dma_start(out=small[:, 2:3], in_=dtb_d[:].unsqueeze(1)))
            do("scalar", lambda e: e.activation(out=small[:, 1:2], in_=small[:, 0:1], func=AF.Exp))
            do("vector", lambda e: e.tensor_scalar(out=small[:, 1:2], in0=small[:, 1:2], scalar1=-1.0, scalar2=None, op0=ALU.mult))

            if mode == "b":
                dma(lambda e: e.dma_start(out=ph8[:, 0:8], in_=phiT_d[:, :]))
                dma(lambda e: e.dma_start(out=ph8[:, 8:16], in_=mask_d[:, :]))
                do("vector", lambda e: e.tensor_tensor(out=ph8[:, 16:24], in0=ph8[:, 0:8], in1=ph8[:, 8:16], op=ALU.mult))
                do("gpsimd", lambda e: e.memset(ph8[:, 24:32], 0.0))
                for j in range(6, -1, -1):
                    do("vector", lambda e, j=j: e.tensor_tensor(out=ph8[:, 24 + j:25 + j], in0=ph8[:, 25 + j:26 + j], in1=ph8[:, 17 + j:18 + j], op=ALU.add))
                do("scalar", lambda e: e.activation(out=ph8[:, 24:32], in_=ph8[:, 24:32], func=AF.Exp))
                do("vector", lambda e: e.tensor_tensor(out=ph8[:, 24:32], in0=ph8[:, 24:32], in1=ph8[:, 8:16], op=ALU.mult))
                do("tensor", lambda e: e.transpose(psF[0:8, 0:64], ph8[0:64, 24:32], identf[0:64, 0:64]))
                do("scalar", lambda e: e.activation(out=ws8[:, :], in_=psF[0:8, 0:64], func=AF.Copy))
                dma(lambda e: e.dma_start(out=ws_scr[:, :], in_=ws8[:, :]))
                dma(lambda e: e.dma_start(out=er63[0:1, :].broadcast_to([1, 64]) if False else ph8[0:1, 0:0] if False else wsrow[0:1, :], in_=ws_scr.rearrange("a b -> (a b)").unsqueeze(0)))
                do("gpsimd", lambda e: e.memset(S_run[:], 0.0))
                for j in range(NCORES):
                    dma(lambda e, j=j: e.dma_start(out=BIG1[:], in_=sf_d[j * 128:(j + 1) * 128, :]))
                    do("tensor", lambda e, j=j: e.matmul(psC[:, 0:64], hp["ones_row"][:], wsrow[0:1, j * 64:(j + 1) * 64], start=True, stop=True))
                    do("scalar", lambda e: e.activation(out=E63[:], in_=psC[:, 0:64], func=AF.Copy))
                    for hh in range(2):
                        do("vector", lambda e, hh=hh: e.tensor_tensor(out=st_tmp[:], in0=BIG1[:, hh * 2048:(hh + 1) * 2048], in1=E63[:, hh * 32:(hh + 1) * 32].unsqueeze(2).broadcast_to([128, 32, 64]), op=ALU.mult))
                        do("vector", lambda e, hh=hh: e.tensor_tensor(out=S_run[:, hh * 2048:(hh + 1) * 2048], in0=S_run[:, hh * 2048:(hh + 1) * 2048], in1=st_tmp[:], op=ALU.add))
            else:
                do("gpsimd", lambda e: e.memset(S_run[:], 0.0))

            for tc in range(2):
                for d in range(ND):
                    dma(lambda e, d=d, tc=tc: e.dma_start(out=XCH[:, d * 512:(d + 1) * 512], in_=x_d[d * 128:(d + 1) * 128, tc * 512:(tc + 1) * 512]))
                emit_ln_fm2(ctx, sem, lambda d, tcc: XCH[:, d * 512:(d + 1) * 512], A32, hp, [tc], 512, 1024)
            for d in range(ND):
                dma(lambda e, d=d: e.dma_start(out=xht[:, d * 4:d * 4 + 3], in_=xh_d[d * 128:(d + 1) * 128, :]))
            for d in range(ND):
                do("scalar", lambda e, d=d: e.activation(out=hp["tb0"][:, 0:3], in_=xht[:, d * 4:d * 4 + 3], func=AF.Identity))
                do("scalar", lambda e, d=d: e.activation(out=hp["tb1"][:, 0:3], in_=xht[:, d * 4:d * 4 + 3], func=AF.Square))
                do("tensor", lambda e, d=d: e.matmul(psA_ln[0:1, 0:3], hp["ones_col_b"][:], hp["tb0"][:, 0:3], start=(d == 0), stop=(d == ND - 1)))
                do("tensor", lambda e, d=d: e.matmul(psA_ln[0:1, 4:7], hp["ones_col_b"][:], hp["tb1"][:, 0:3], start=(d == 0), stop=(d == ND - 1)))
            R = hp["row"]
            do("vector", lambda e: e.tensor_scalar(out=R[0:1, 0:3], in0=psA_ln[0:1, 0:3], scalar1=1.0 / D, scalar2=None, op0=ALU.mult))
            do("vector", lambda e: e.tensor_scalar(out=R[0:1, 4:7], in0=psA_ln[0:1, 4:7], scalar1=1.0 / D, scalar2=None, op0=ALU.mult))
            do("vector", lambda e: e.tensor_tensor(out=R[0:1, 8:11], in0=R[0:1, 0:3], in1=R[0:1, 0:3], op=ALU.mult))
            do("vector", lambda e: e.tensor_tensor(out=R[0:1, 4:7], in0=R[0:1, 4:7], in1=R[0:1, 8:11], op=ALU.subtract))
            do("vector", lambda e: e.tensor_scalar(out=R[0:1, 4:7], in0=R[0:1, 4:7], scalar1=1e-5, scalar2=None, op0=ALU.add))
            do("scalar", lambda e: e.activation(out=R[0:1, 4:7], in_=R[0:1, 4:7], func=AF.Ln))
            do("scalar", lambda e: e.activation(out=R[0:1, 4:7], in_=R[0:1, 4:7], func=AF.Exp, scale=-0.5))
            do("vector", lambda e: e.tensor_tensor(out=R[0:1, 8:11], in0=R[0:1, 0:3], in1=R[0:1, 4:7], op=ALU.mult))
            do("vector", lambda e: e.tensor_scalar(out=R[0:1, 8:11], in0=R[0:1, 8:11], scalar1=-1.0, scalar2=None, op0=ALU.mult))
            do("tensor", lambda e: e.matmul(psB_ln[:, 0:3], hp["ones_row"][:], R[0:1, 4:7], start=True, stop=True))
            do("scalar", lambda e: e.activation(out=hp["abc"][:, 0:3], in_=psB_ln[:, 0:3], func=AF.Copy))
            do("tensor", lambda e: e.matmul(psB_ln[:, 0:3], hp["ones_row"][:], R[0:1, 8:11], start=True, stop=True))
            do("scalar", lambda e: e.activation(out=hp["bbc"][:, 0:3], in_=psB_ln[:, 0:3], func=AF.Copy))
            for d in range(ND):
                do("vector", lambda e, d=d: e.tensor_tensor(out=hp["tf"][:, 0:3], in0=xht[:, d * 4:d * 4 + 3], in1=hp["abc"][:, 0:3], op=ALU.mult))
                do("vector", lambda e, d=d: e.tensor_tensor(out=lnxh[:, d * 4:d * 4 + 3], in0=hp["tf"][:, 0:3], in1=hp["bbc"][:, 0:3], op=ALU.add))

            ot_list = (list(range(32)) if mode == "b" else []) + list(range(32, 67))
            for oc in ot_list:
                ncols = 64 if oc == 66 else 128
                dma(lambda e, oc=oc, ncols=ncols: e.dma_start(
                    out=BIG1[:, 0:16 * ncols].rearrange("p (a c) -> p a c", c=ncols),
                    in_=ipT_d[:, oc * 128: oc * 128 + ncols].rearrange("(a p) c -> p a c", p=128)))
                do("vector", lambda e, ncols=ncols: e.tensor_copy(out=BIG2[:, 0:16 * ncols], in_=BIG1[:, 0:16 * ncols]))
                for tc in range(2):
                    chain("tensor", [
                        (lambda e, d=d, tc=tc, ncols=ncols: e.matmul(PSA[0:ncols, tc * 512:tc * 512 + 512], BIG2[:, d * ncols:(d + 1) * ncols], A32[:, d * 1024 + tc * 512: d * 1024 + tc * 512 + 512], start=(d == 0), stop=(d == ND - 1)))
                        for d in range(ND)])
                if oc < 32:
                    do("scalar", lambda e: e.activation(out=zt[:, 0:512], in_=PSA[:, 0:512], func=AF.Silu))
                    do("scalar", lambda e: e.activation(out=zt[:, 512:1024], in_=PSA[:, 512:1024], func=AF.Silu))
                    for tt in range(8):
                        do("tensor", lambda e, tt=tt: e.transpose(psT[:], zt[:, tt * 128:(tt + 1) * 128], identb[:]))
                        do("vector", lambda e, tt=tt: e.tensor_copy(out=xbs[:, 0:128], in_=psT[:]))
                        dma(lambda e, tt=tt, oc=oc: e.dma_start(out=z_scr[tt * 128:(tt + 1) * 128, oc * 128:(oc + 1) * 128], in_=xbs[:, 0:128]), which="gpsimd")
                elif oc < 66:
                    ct = oc - 32
                    chain("tensor", [
                        (lambda e, d=d: e.matmul(PSB[0:128, 0:3], BIG2[:, d * 128:(d + 1) * 128], lnxh[:, d * 4:d * 4 + 3], start=(d == 0), stop=(d == ND - 1)))
                        for d in range(ND)])
                    do("scalar", lambda e: e.activation(out=xbcr[:, 0:3], in_=PSB[0:128, 0:3], func=AF.Copy))
                    do("scalar", lambda e: e.activation(out=xbcr[:, 3:515], in_=PSA[:, 0:512], func=AF.Copy))
                    do("scalar", lambda e: e.activation(out=xbcr[:, 515:1027], in_=PSA[:, 512:1024], func=AF.Copy))
                    do("vector", lambda e, ct=ct: e.tensor_scalar(out=xbs[:], in0=xbcr[:, 0:1024], scalar1=cw_sb[:, ct * 4:ct * 4 + 1], scalar2=None, op0=ALU.mult))
                    for j in range(1, 4):
                        do("vector", lambda e, ct=ct, j=j: e.scalar_tensor_tensor(out=xbs[:], in0=xbcr[:, j:j + 1024], scalar=cw_sb[:, ct * 4 + j:ct * 4 + j + 1], in1=xbs[:], op0=ALU.mult, op1=ALU.add))
                    do("scalar", lambda e, ct=ct: e.activation(out=xbs[:], in_=xbs[:], func=AF.Silu, bias=cb_sb[:, ct:ct + 1]))
                    if ct < 32:
                        for tt in range(8):
                            do("tensor", lambda e, tt=tt: e.transpose(psT[:], xbs[:, tt * 128:(tt + 1) * 128], identb[:]))
                            do("vector", lambda e, tt=tt: e.tensor_copy(out=xbcr[:, 0:128], in_=psT[:]))
                            dma(lambda e, tt=tt, ct=ct: e.dma_start(out=xssm_scr[tt * 128:(tt + 1) * 128, ct * 128:(ct + 1) * 128], in_=xbcr[:, 0:128]), which="gpsimd")
                    else:
                        do("vector", lambda e, ct=ct: e.tensor_copy(out=BC[:, (ct - 32) * 1024:(ct - 31) * 1024], in_=xbs[:]))
                else:
                    do("scalar", lambda e: e.activation(out=dt_fm[:, 0:512], in_=PSA[0:64, 0:512], func=AF.Identity, bias=small[:, 2:3]))
                    do("scalar", lambda e: e.activation(out=dt_fm[:, 512:1024], in_=PSA[0:64, 512:1024], func=AF.Identity, bias=small[:, 2:3]))
                    do("scalar", lambda e: e.activation(out=adisc[:, 0:1024], in_=dt_fm[:], func=AF.Abs))
                    do("scalar", lambda e: e.activation(out=adisc[:, 0:1024], in_=adisc[:, 0:1024], func=AF.Exp, scale=-1.0))
                    do("vector", lambda e: e.tensor_scalar(out=adisc[:, 0:1024], in0=adisc[:, 0:1024], scalar1=1.0, scalar2=None, op0=ALU.add))
                    do("scalar", lambda e: e.activation(out=adisc[:, 0:1024], in_=adisc[:, 0:1024], func=AF.Ln))
                    do("vector", lambda e: e.tensor_scalar(out=dt_fm[:], in0=dt_fm[:], scalar1=0.0, scalar2=None, op0=ALU.max))
                    do("vector", lambda e: e.tensor_tensor(out=dt_fm[:], in0=dt_fm[:], in1=adisc[:, 0:1024], op=ALU.add))
            do("vector", lambda e: e.tensor_scalar(out=adisc[:], in0=dt_fm[:], scalar1=small[:, 1:2], scalar2=None, op0=ALU.mult))
            for c in range(NCHUNK):
                do("vector", lambda e, c=c: e.tensor_tensor_scan(out=acs[:, c * 64:(c + 1) * 64], data0=adisc[:, c * 64:(c + 1) * 64], data1=adisc[:, c * 64:(c + 1) * 64], initial=0.0, op0=ALU.add, op1=ALU.bypass))
            if mode == "a":
                do("vector", lambda e: e.reduce_sum(out=small[:, 3:4], in_=adisc[:], axis=AX.X))
                dma(lambda e: e.dma_start(out=phi_d[:, :], in_=small[:, 3:4]))
            if mode == "b":
                dma(lambda e: e.dma_start(out=er63[0:1, :], in_=dvec_d[:].unsqueeze(0)))
                do("tensor", lambda e: e.matmul(psC[:, 0:64], hp["ones_row"][:], er63[0:1, :], start=True, stop=True))
                do("scalar", lambda e: e.activation(out=D64[:], in_=psC[:, 0:64], func=AF.Copy))

            for c in range(NCHUNK):
                ac = acs[:, c * 64:(c + 1) * 64]
                dtc = dt_fm[:, c * 64:(c + 1) * 64]
                dma(lambda e, c=c: e.dma_start(out=acs_scr[:, :], in_=ac))
                do("tensor", lambda e: e.transpose(psF[0:64, 0:64], ac, identf[0:64, 0:64]))
                do("scalar", lambda e: e.activation(out=acsT[:], in_=psF[0:64, 0:64], func=AF.Copy))
                do("tensor", lambda e: e.transpose(psF[0:64, 0:64], dtc, identf[0:64, 0:64]))
                do("scalar", lambda e: e.activation(out=dtT[:], in_=psF[0:64, 0:64], func=AF.Copy))
                if mode == "b":
                    do("scalar", lambda e: e.activation(out=eacsT[:], in_=acsT[:], func=AF.Exp))
                do("vector", lambda e: e.tensor_tensor(out=ad63[:], in0=ac[:, 63:64].broadcast_to([64, 64]), in1=ac, op=ALU.subtract))
                do("scalar", lambda e: e.activation(out=ad63[:], in_=ad63[:], func=AF.Exp))
                do("vector", lambda e: e.tensor_tensor(out=ad63[:], in0=ad63[:], in1=dtc, op=ALU.mult))
                do("tensor", lambda e: e.transpose(psF[0:64, 0:64], ad63[:], identf[0:64, 0:64]))
                do("scalar", lambda e: e.activation(out=w_sh[:], in_=psF[0:64, 0:64], func=AF.Copy))
                do("scalar", lambda e: e.activation(out=small[:, 4:5], in_=ac[:, 63:64], func=AF.Exp))
                do("tensor", lambda e: e.transpose(psF[0:1, 0:64], small[:, 4:5], identf[0:64, 0:64]))
                do("scalar", lambda e: e.activation(out=er63[0:1, :], in_=psF[0:1, 0:64], func=AF.Copy))
                do("tensor", lambda e: e.matmul(psC[:, 0:64], hp["ones_row"][:], er63[0:1, :], start=True, stop=True))
                do("scalar", lambda e: e.activation(out=E63[:], in_=psC[:, 0:64], func=AF.Copy))
                do("tensor", lambda e, c=c: e.transpose(psT[0:64, 0:128], BC[:, c * 64:(c + 1) * 64], identb[:]))
                do("vector", lambda e: e.tensor_copy(out=btm[:], in_=psT[0:64, :]))
                if mode == "b":
                    do("tensor", lambda e, c=c: e.matmul(psF[0:64, 0:64], BC[:, c * 64:(c + 1) * 64], BC[:, 1024 + c * 64: 1024 + (c + 1) * 64], start=True, stop=True))
                    do("scalar", lambda e: e.activation(out=gt[:], in_=psF[0:64, 0:64], func=AF.Copy))
                    do("gpsimd", lambda e: e.affine_select(out=gt[:], in_=gt[:], compare_op=ALU.is_ge, fill=0.0, base=0, pattern=[[1, 64]], channel_multiplier=-1))
                    dma(lambda e: e.dma_start(out=BIG1[0:64, :].rearrange("s (h l) -> s h l", h=64), in_=AP(tensor=acs_scr.tensor, offset=0, ap=[[0, 64], [64, 64], [1, 64]])))
                    do("vector", lambda e: e.tensor_tensor(out=BIG1[0:64, :], in0=BIG1[0:64, :], in1=acsT[:].unsqueeze(2).broadcast_to([64, 64, 64]), op=ALU.subtract))
                    do("vector", lambda e: e.tensor_scalar(out=BIG1[0:64, :], in0=BIG1[0:64, :], scalar1=0.0, scalar2=None, op0=ALU.min))
                    do("scalar", lambda e: e.activation(out=BIG4[0:64, :], in_=BIG1[0:64, :], func=AF.Exp))
                    do("gpsimd", lambda e: e.tensor_tensor(out=BIG4[0:64, :], in0=BIG4[0:64, :], in1=gt[:].unsqueeze(1).broadcast_to([64, 64, 64]), op=ALU.mult))
                dma(lambda e, c=c: e.dma_start(out=BIG2[0:64, :], in_=xssm_scr[c * 64:(c + 1) * 64, :]))
                do("vector", lambda e: e.tensor_tensor(out=BIG2[0:64, :], in0=BIG2[0:64, :], in1=dtT[:].unsqueeze(2).broadcast_to([64, 64, 64]), op=ALU.mult))
                do("vector", lambda e: e.tensor_tensor(out=BIG3[0:64, :], in0=BIG2[0:64, :], in1=w_sh[:].unsqueeze(2).broadcast_to([64, 64, 64]), op=ALU.mult))
                if mode == "b":
                    do("scalar", lambda e: e.activation(out=syb[:], in_=S_run[:], func=AF.Copy))
                    for hh in range(2):
                        hc = hh * 2048
                        mms = []
                        for hl in range(32):
                            h = hh * 32 + hl
                            mms.append(lambda e, h=h, hl=hl: e.matmul(PSA[0:64, hl * 64:(hl + 1) * 64], BIG4[0:64, h * 64:(h + 1) * 64], BIG2[0:64, h * 64:(h + 1) * 64], start=True, stop=True))
                        chain("tensor", mms)
                        chain("tensor", [
                            (lambda e, i=i, c=c, hc=hc: e.matmul(PSB[0:64, i * 512:(i + 1) * 512], BC[:, 1024 + c * 64:1024 + (c + 1) * 64], syb[:, hc + i * 512: hc + (i + 1) * 512], start=True, stop=True))
                            for i in range(4)])
                        do("vector", lambda e, hh=hh: e.tensor_tensor(out=st_tmp[0:64, :], in0=PSB[0:64, 0:2048], in1=eacsT[:, hh * 32:(hh + 1) * 32].unsqueeze(2).broadcast_to([64, 32, 64]), op=ALU.mult))
                        do("vector", lambda e, c=c, hc=hc: e.tensor_tensor(out=y_tile[(c % 2) * 64:(c % 2) * 64 + 64, hc:hc + 2048], in0=PSA[0:64, 0:2048], in1=st_tmp[0:64, :], op=ALU.add))
                for hh in range(2):
                    hc = hh * 2048
                    chain("tensor", [
                        (lambda e, i=i, hc=hc: e.matmul(PSA[:, i * 512:(i + 1) * 512], btm[:], BIG3[0:64, hc + i * 512: hc + (i + 1) * 512], start=True, stop=True))
                        for i in range(4)])
                    do("vector", lambda e, hh=hh: e.tensor_tensor(out=st_tmp[:], in0=S_run[:, hc:hc + 2048], in1=E63[:, hh * 32:(hh + 1) * 32].unsqueeze(2).broadcast_to([128, 32, 64]), op=ALU.mult))
                    do("vector", lambda e, hc=hc: e.tensor_tensor(out=S_run[:, hc:hc + 2048], in0=st_tmp[:], in1=PSA[:, 0:2048], op=ALU.add))
                if mode == "b" and c % 2 == 1:
                    tt = c // 2
                    xsb = BIG1[:, 0:2048].bitcast(BF16)
                    dma(lambda e, tt=tt: e.dma_start(out=xsb, in_=xssm_scr[tt * 128:(tt + 1) * 128, :]))
                    do("vector", lambda e: e.tensor_tensor(out=xsb, in0=xsb, in1=D64[:].unsqueeze(2).broadcast_to([128, 64, 64]), op=ALU.mult))
                    do("vector", lambda e: e.tensor_tensor(out=y_tile[:], in0=y_tile[:], in1=xsb, op=ALU.add))
                    zl = BIG3[:, 0:4096]
                    dma(lambda e, tt=tt: e.dma_start(out=zl, in_=z_scr[tt * 128:(tt + 1) * 128, :]))
                    do("vector", lambda e: e.tensor_tensor(out=y_tile[:], in0=y_tile[:], in1=zl, op=ALU.mult))
                    sqb = BIG2[:, 0:4096]
                    do("scalar", lambda e: e.activation(out=sqb, in_=y_tile[:], func=AF.Square))
                    do("vector", lambda e: e.reduce_sum(out=rs_col[:, 0:1], in_=sqb, axis=AX.X))
                    do("vector", lambda e: e.tensor_scalar(out=rs_col[:, 0:1], in0=rs_col[:, 0:1], scalar1=1.0 / DIN, scalar2=1e-5, op0=ALU.mult, op1=ALU.add))
                    do("scalar", lambda e: e.activation(out=rs_col[:, 1:2], in_=rs_col[:, 0:1], func=AF.Ln))
                    do("scalar", lambda e: e.activation(out=rs_col[:, 1:2], in_=rs_col[:, 1:2], func=AF.Exp, scale=-0.5))
                    do("scalar", lambda e: e.activation(out=y_tile[:], in_=y_tile[:], func=AF.Identity, scale=rs_col[:, 1:2]))
                    nwb = BIG3[:, 0:4096]
                    dma(lambda e: e.dma_start(out=nwb.rearrange("p (a c) -> p a c", a=1), in_=AP(tensor=nw_d.tensor, offset=0, ap=[[0, 128], [0, 1], [1, 4096]])), which="gpsimd")
                    do("vector", lambda e: e.tensor_tensor(out=y_tile[:], in0=y_tile[:], in1=nwb, op=ALU.mult))
                    for k in range(32):
                        do("tensor", lambda e, k=k: e.transpose(psT[:], y_tile[:, k * 128:(k + 1) * 128], identb[:]))
                        do("vector", lambda e, k=k, tt=tt: e.tensor_copy(out=A32[:, k * 512 + (tt % 4) * 128: k * 512 + (tt % 4) * 128 + 128], in_=psT[:]))
                if mode == "b" and c in (7, 15):
                    tchunk = 0 if c == 7 else 1
                    for ft in range(ND):
                        dma(lambda e, ft=ft: e.dma_start(
                            out=BIG1[:].rearrange("p (a c2) -> p a c2", c2=128),
                            in_=opT_d[:, ft * 128:(ft + 1) * 128].rearrange("(a p) c2 -> p a c2", p=128)))
                        do("vector", lambda e: e.tensor_copy(out=BIG2[:], in_=BIG1[:]))
                        chain("tensor", [
                            (lambda e, k=k: e.matmul(psC[:], BIG2[:, k * 128:(k + 1) * 128], A32[:, k * 512:(k + 1) * 512], start=(k == 0), stop=(k == 31)))
                            for k in range(32)])
                        dma(lambda e, ft=ft, tchunk=tchunk: e.dma_start(out=x2s[:], in_=x_d[ft * 128:(ft + 1) * 128, tchunk * 512:(tchunk + 1) * 512]))
                        do("vector", lambda e: e.tensor_tensor(out=x2s[:], in0=x2s[:], in1=psC[:], op=ALU.add))
                        dma(lambda e, ft=ft, tchunk=tchunk: e.dma_start(out=x3_d[ft * 128:(ft + 1) * 128, tchunk * 512:(tchunk + 1) * 512], in_=x2s[:]))
            if mode == "a":
                dma(lambda e: e.dma_start(out=sfin_d[:, :], in_=S_run[:]))
            if mode == "b":
                do("vector", lambda e: e.tensor_copy(out=XCH[0:64, 0:1024], in_=dt_fm[:]))
                do("vector", lambda e: e.tensor_copy(out=XCH[0:64, 1024:2048], in_=acs[:]))
                dma(lambda e: e.dma_start(out=dbgm_d[:, 0:2048], in_=XCH[:, 0:2048]))
            ctx.wait("sync", sem, ctx.val(sem))
        return f

    with nc.Block() as block:
        for en in ENGS:
            getattr(block, en)(prog(en))
    st.close()
    return nc


NF1 = HID1 // 128


def build_u6():
    nc = bass.Bass("TRN2", target_bir_lowering=False)
    x_d = nc.dram_tensor("x", [D, TOK], F32, kind="ExternalInput").ap()
    wgT_d = nc.dram_tensor("wgT", [D, HID1], I32, kind="ExternalInput").ap()
    wuT_d = nc.dram_tensor("wuT", [D, HID1], I32, kind="ExternalInput").ap()
    wdT_d = nc.dram_tensor("wdT", [HID1, D], I32, kind="ExternalInput").ap()
    sg_d = nc.dram_tensor("sg", [HID1], F32, kind="ExternalInput").ap()
    su_d = nc.dram_tensor("su", [HID1], F32, kind="ExternalInput").ap()
    sd_d = nc.dram_tensor("sd", [D], F32, kind="ExternalInput").ap()
    x4_d = nc.dram_tensor("x4", [D, TOK], F32, kind="ExternalOutput").ap()

    st = contextlib.ExitStack()
    sb = lambda *a, **k: st.enter_context(nc.sbuf_tensor(*a, **k))
    ps = lambda *a, **k: st.enter_context(nc.psum_tensor(*a, **k))
    A = sb("A", [128, ND * TOK], BF16)
    GU = sb("GU", [128, NF1 * TOK], BF16)
    XCH = GU[:, 0:16384].bitcast(F32)
    wcol_i = sb("wcol_i", [128, 2048], I32)
    wcol_b = sb("wcol_b", [128, 2048], BF16)
    wdcol_b = sb("wdcol_b", [128, NF1 * 128], BF16)
    gt_sb = sb("gt_sb", [128, 512], BF16)
    tf2 = sb("tf2", [128, 512], F32)
    scol = sb("scol", [128, 2 * NF1 + ND], F32)
    x2s = sb("x2s", [128, 512], F32)
    hp = {
        "tb0": sb("tb0", [128, 512], BF16),
        "tb1": sb("tb1", [128, 512], BF16),
        "tf": sb("tf", [128, 512], F32),
        "abc": sb("abc", [128, 512], F32),
        "bbc": sb("bbc", [128, 512], F32),
        "row": sb("row", [1, 1536], F32),
        "ones_col_b": sb("ones_col_b", [128, 1], BF16),
        "ones_row": sb("ones_row", [1, 128], F32),
    }
    PSA = ps("PSA", [128, 2048], F32)
    psC = PSA[:, 0:512]
    psC2 = PSA[:, 512:1024]
    hp["psA"] = PSA[0:1, 1024:2048]
    hp["psB"] = PSA[:, 1536:2048]
    sem = st.enter_context(nc.semaphore("s"))

    def prog(eng_name):
        def f(eng):
            ctx = Ctx(eng_name, eng)
            w = lambda: [(sem, ctx.val(sem))]

            def chain(which, fns, inc_last=1):
                if ctx.en == which:
                    ctx.eng.wait_ge(sem, ctx.val(sem))
                    for fn in fns:
                        ins = fn(ctx.eng)
                    ins.then_inc(sem, inc_last)
                ctx._bump(sem, inc_last)

            def do(which, f2):
                ctx.do(which, f2, sem=sem, wait=w())

            def dma(f2, which="sync"):
                ctx.do(which, f2, sem=sem, inc=16, wait=w())

            do("gpsimd", lambda e: e.memset(hp["ones_col_b"][:], 1.0))
            do("gpsimd", lambda e: e.memset(hp["ones_row"][:], 1.0))
            for ff in range(NF1):
                dma(lambda e, ff=ff: e.dma_start(out=scol[:, ff:ff + 1], in_=sg_d[ff * 128:(ff + 1) * 128].unsqueeze(1)))
                dma(lambda e, ff=ff: e.dma_start(out=scol[:, NF1 + ff:NF1 + ff + 1], in_=su_d[ff * 128:(ff + 1) * 128].unsqueeze(1)))
            for ft in range(ND):
                dma(lambda e, ft=ft: e.dma_start(out=scol[:, 2 * NF1 + ft:2 * NF1 + ft + 1], in_=sd_d[ft * 128:(ft + 1) * 128].unsqueeze(1)))
            for tc in range(2):
                for d in range(ND):
                    dma(lambda e, d=d, tc=tc: e.dma_start(out=XCH[:, d * 512:(d + 1) * 512], in_=x_d[d * 128:(d + 1) * 128, tc * 512:(tc + 1) * 512]))
                emit_ln_fm2(ctx, sem, lambda d, tcc: XCH[:, d * 512:(d + 1) * 512], A, hp, [tc], 512, 1024)
            for ff in range(NF1):
                for (wT, sc_off, func) in ((wgT_d, 0, AF.Silu), (wuT_d, NF1, AF.Identity)):
                    dma(lambda e, ff=ff, wT=wT: e.dma_start(
                        out=wcol_i[:].rearrange("p (a c) -> p a c", c=128),
                        in_=wT[:, ff * 128:(ff + 1) * 128].rearrange("(a p) c -> p a c", p=128)))
                    do("scalar", lambda e: e.activation(out=wcol_b[:], in_=wcol_i[:], func=AF.Copy, bias=-128.0))
                    for tc in range(2):
                        chain("tensor", [
                            (lambda e, d=d, tc=tc: e.matmul(psC if tc == 0 else psC2, wcol_b[:, d * 128:(d + 1) * 128], A[:, d * 1024 + tc * 512: d * 1024 + tc * 512 + 512], start=(d == 0), stop=(d == ND - 1)))
                            for d in range(ND)])
                    if sc_off == 0:
                        for tc in range(2):
                            do("scalar", lambda e, ff=ff, tc=tc: e.activation(out=GU[:, ff * 1024 + tc * 512: ff * 1024 + tc * 512 + 512], in_=(psC if tc == 0 else psC2), func=AF.Silu, scale=scol[:, ff:ff + 1]))
                    else:
                        for tc in range(2):
                            do("scalar", lambda e, ff=ff, tc=tc: e.activation(out=gt_sb[:], in_=(psC if tc == 0 else psC2), func=AF.Identity, scale=scol[:, NF1 + ff:NF1 + ff + 1]))
                            do("vector", lambda e, ff=ff, tc=tc: e.tensor_tensor(out=GU[:, ff * 1024 + tc * 512: ff * 1024 + tc * 512 + 512], in0=GU[:, ff * 1024 + tc * 512: ff * 1024 + tc * 512 + 512], in1=gt_sb[:], op=ALU.mult))
            for ft in range(ND):
                for sub in range(4):
                    dma(lambda e, ft=ft, sub=sub: e.dma_start(
                        out=wcol_i[:, 0:1408].rearrange("p (a c) -> p a c", c=128),
                        in_=wdT_d[sub * 1408:(sub + 1) * 1408, ft * 128:(ft + 1) * 128].rearrange("(a p) c -> p a c", p=128)))
                    do("scalar", lambda e, sub=sub: e.activation(out=wdcol_b[:, sub * 1408:(sub + 1) * 1408], in_=wcol_i[:, 0:1408], func=AF.Copy, bias=-128.0))
                for tc in range(2):
                    chain("tensor", [
                        (lambda e, f=f2, tc=tc: e.matmul(psC if tc == 0 else psC2, wdcol_b[:, f * 128:(f + 1) * 128], GU[:, f * 1024 + tc * 512: f * 1024 + tc * 512 + 512], start=(f == 0), stop=(f == NF1 - 1)))
                        for f2 in range(NF1)])
                    do("scalar", lambda e, ft=ft, tc=tc: e.activation(out=tf2[:], in_=(psC if tc == 0 else psC2), func=AF.Identity, scale=scol[:, 2 * NF1 + ft:2 * NF1 + ft + 1]))
                    dma(lambda e, ft=ft, tc=tc: e.dma_start(out=x2s[:], in_=x_d[ft * 128:(ft + 1) * 128, tc * 512:(tc + 1) * 512]))
                    do("vector", lambda e: e.tensor_tensor(out=x2s[:], in0=x2s[:], in1=tf2[:], op=ALU.add))
                    dma(lambda e, ft=ft, tc=tc: e.dma_start(out=x4_d[ft * 128:(ft + 1) * 128, tc * 512:(tc + 1) * 512], in_=x2s[:]))
            ctx.wait("sync", sem, ctx.val(sem))
        return f

    with nc.Block() as block:
        for en in ENGS:
            getattr(block, en)(prog(en))
    st.close()
    return nc


_BF = ml_dtypes.bfloat16


def kernel(**inputs):
    global _DBG
    x = np.ascontiguousarray(np.asarray(inputs["L0_attn_hidden_states"]).reshape(S, D))
    if _X2_OVERRIDE is not None:
        x2_full = _X2_OVERRIDE
        return _run_u5(inputs, x2_full)
    nc1 = build_u1()
    in_maps = [{"x": np.ascontiguousarray(x[c * TOK:(c + 1) * TOK].T)} for c in range(NCORES)]
    res = run_bass_kernel_spmd(nc1, in_maps, core_ids=list(range(NCORES)))
    lnx_full = np.concatenate([res.results[c]["lnx"] for c in range(NCORES)], axis=1)
    if _STAGE == "u1":
        return lnx_full

    wqT = np.ascontiguousarray(np.asarray(inputs["L0_attn_Wq"]).T)
    wkT = np.ascontiguousarray(np.asarray(inputs["L0_attn_Wk"]).T)
    wvT = np.ascontiguousarray(np.asarray(inputs["L0_attn_Wv"]).T)
    bq = np.asarray(inputs["L0_attn_bq"]); bk = np.asarray(inputs["L0_attn_bk"]); bv = np.asarray(inputs["L0_attn_bv"])
    nc2 = build_u2()
    in_maps = []
    for c in range(NCORES):
        sl = slice(c * HPC * HD, (c + 1) * HPC * HD)
        in_maps.append({
            "lnx": lnx_full,
            "wqT": np.ascontiguousarray(wqT[:, sl]),
            "wkT": np.ascontiguousarray(wkT[:, sl]),
            "wvT": np.ascontiguousarray(wvT[:, sl]),
            "bq": np.ascontiguousarray(bq[sl]),
            "bk": np.ascontiguousarray(bk[sl]),
            "bv": np.ascontiguousarray(bv[sl]),
        })
    res = run_bass_kernel_spmd(nc2, in_maps, core_ids=list(range(NCORES)))
    attn_full = np.concatenate([res.results[c]["attn"] for c in range(NCORES)], axis=0)
    if _STAGE == "u2":
        return attn_full

    woT = np.ascontiguousarray(np.asarray(inputs["L0_attn_Wo"]).T)
    bo = np.asarray(inputs["L0_attn_bo"])
    w1T = np.ascontiguousarray(np.asarray(inputs["L0_ffn_w1"]).T)
    b1 = np.asarray(inputs["L0_ffn_b1"])
    w2T = np.ascontiguousarray(np.asarray(inputs["L0_ffn_w2"]).T)
    b2 = np.asarray(inputs["L0_ffn_b2"])
    nc3 = build_u3()
    in_maps = []
    for c in range(NCORES):
        in_maps.append({
            "x": np.ascontiguousarray(x[c * TOK:(c + 1) * TOK].T),
            "attn": np.ascontiguousarray(attn_full[:, c * TOK:(c + 1) * TOK]),
            "woT": woT, "bo": bo, "w1T": w1T, "b1": b1, "w2T": w2T, "b2": b2,
        })
    res = run_bass_kernel_spmd(nc3, in_maps, core_ids=list(range(NCORES)))
    x2_full = np.concatenate([res.results[c]["x2"] for c in range(NCORES)], axis=1)
    _DBG = {
        "x1": np.concatenate([res.results[c]["dbg"] for c in range(NCORES)], axis=1),
        "thr": res.results[0]["dbg2"],
        "h": res.results[0]["dbg3"],
    }
    if _STAGE == "u3":
        return x2_full
    return _run_u5(inputs, x2_full)


def _run_u5(inputs, x2_full):
    global _DBG
    ipT = np.ascontiguousarray(np.asarray(inputs["L1_attn_in_proj_w"]).T)
    cw = np.ascontiguousarray(np.asarray(inputs["L1_attn_conv_w"]).reshape(DXBC, 4))
    cb = np.asarray(inputs["L1_attn_conv_b"])
    alog = np.asarray(inputs["L1_attn_A_log"])
    dtb = np.asarray(inputs["L1_attn_dt_bias"])
    dvec = np.asarray(inputs["L1_attn_D"])
    nw = np.asarray(inputs["L1_attn_norm_w"])
    opT = np.ascontiguousarray(np.asarray(inputs["L1_attn_out_proj_w"]).T)
    zero_h = np.zeros((D, 3), np.float32)
    nc5a = build_u5("a")
    in_maps = []
    for c in range(NCORES):
        xs = np.ascontiguousarray(x2_full[:, c * TOK:(c + 1) * TOK])
        xh = zero_h if c == 0 else np.ascontiguousarray(x2_full[:, c * TOK - 3:c * TOK])
        in_maps.append({"x": xs, "xh": xh, "ipT": ipT, "cw": cw, "cb": cb, "alog": alog, "dtb": dtb})
    res = run_bass_kernel_spmd(nc5a, in_maps, core_ids=list(range(NCORES)))
    sf = np.concatenate([res.results[c]["sfin"] for c in range(NCORES)], axis=0)
    phiT = np.concatenate([res.results[c]["phi"] for c in range(NCORES)], axis=1)
    if _STAGE == "u5a":
        return sf, phiT

    nc5b = build_u5("b")
    in_maps = []
    for c in range(NCORES):
        xs = np.ascontiguousarray(x2_full[:, c * TOK:(c + 1) * TOK])
        xh = zero_h if c == 0 else np.ascontiguousarray(x2_full[:, c * TOK - 3:c * TOK])
        maskr = np.ascontiguousarray(np.tile((np.arange(NCORES) < c).astype(np.float32), (NH_M, 1)))
        in_maps.append({"x": xs, "xh": xh, "ipT": ipT, "cw": cw, "cb": cb, "alog": alog, "dtb": dtb,
                        "dvec": dvec, "nw": nw, "opT": opT, "sf": sf, "phiT": np.ascontiguousarray(phiT), "maskr": maskr})
    res = run_bass_kernel_spmd(nc5b, in_maps, core_ids=list(range(NCORES)))
    x3_full = np.concatenate([res.results[c]["x3"] for c in range(NCORES)], axis=1)
    _DBG = {"m": res.results[0]["dbgm"], "x2": x2_full}
    if _STAGE == "u5":
        return x3_full

    wgT = np.ascontiguousarray(np.asarray(inputs["L1_ffn_wq_gate"]).T)
    wuT = np.ascontiguousarray(np.asarray(inputs["L1_ffn_wq_up"]).T)
    wdT = np.ascontiguousarray(np.asarray(inputs["L1_ffn_wq_down"]).T)
    sg = np.asarray(inputs["L1_ffn_s_gate"]); su = np.asarray(inputs["L1_ffn_s_up"]); sd = np.asarray(inputs["L1_ffn_s_down"])
    nc6 = build_u6()
    in_maps = []
    for c in range(NCORES):
        in_maps.append({"x": np.ascontiguousarray(x3_full[:, c * TOK:(c + 1) * TOK]),
                        "wgT": wgT, "wuT": wuT, "wdT": wdT, "sg": sg, "su": su, "sd": sd})
    res = run_bass_kernel_spmd(nc6, in_maps, core_ids=list(range(NCORES)))
    x4_full = np.concatenate([res.results[c]["x4"] for c in range(NCORES)], axis=1)
    if _STAGE == "u6":
        return x4_full
    return np.ascontiguousarray(x4_full.T).reshape(1, S, D)


_STAGE = "full"
_DBG = None
_X2_OVERRIDE = None
